# version 01 (final)
# baseline (speedup 1.0000x reference)
import time
import contextlib
import numpy as np
import concourse.bass as bass
import concourse.mybir as mybir
from concourse.bass_utils import run_bass_kernel_spmd

F32 = mybir.dt.float32
BF16 = mybir.dt.bfloat16
I32 = mybir.dt.int32
AF = mybir.ActivationFunctionType
ALU = mybir.AluOpType
AX = mybir.AxisListType.X

EMBED = 2048
D_INNER = 4096
NHEADS = 32
HEADDIM = 128
DSTATE = 64
CHUNK = 256
D_IN_PROJ = 8352
E_PAD = 8448
NE = 66
NE_XBC0 = 32
NE_XBC1 = 65
NK = 16
SEG = 1024
SEGH = 1152
NCH = 4
HID = 8192
EPS_LN = 1e-5
EPS_RMS = 1e-6

C_ID = 0
C_LT = 128
C_ON = 256
C_UT = 512

ARENA = 34000
DEBUG_TAPS = []
STAGE_LIMIT = None
MF_PARTS = 3
MF_SUB = None
MF_FLAGS = set()
SKIP_LN = False
SKIP_SETUP = False



_FENCE_ALWAYS = {"reduce_sum", "reduce_max", "tensor_reduce", "reciprocal",
                 "tensor_tensor_scan", "bn_stats", "bn_aggr"}
_WRAP = _FENCE_ALWAYS | {"tensor_copy", "tensor_tensor", "tensor_scalar",
                         "tensor_add", "tensor_sub", "tensor_mul", "memset",
                         "scalar_tensor_tensor", "activation", "copy", "mul",
                         "add", "square", "sqrt", "matmul", "transpose",
                         "tensor_scalar_mul", "tensor_scalar_add", "nop"}


class _InsShim:
    def __init__(self, proxy, ins):
        self._p = proxy
        self._i = ins

    def then_inc(self, sem, n=1):
        p = self._p
        if id(self._i) in p._marked:
            p.real_nop_fn().then_inc(sem, n)
        else:
            self._i.then_inc(sem, n)
            p._marked[id(self._i)] = self._i
        return self


class EngProxy:

    def __init__(self, eng, fence_sem, is_pe, real_nop_fn=None):
        self._eng = eng
        self._fs = fence_sem
        self._fc = 0
        self._is_pe = is_pe
        self.last = None
        self._marked = {}
        self.real_nop_fn = real_nop_fn or (lambda: eng.nop())

    def __getattr__(self, name):
        fn = getattr(self._eng, name)
        if name not in _WRAP:
            return fn

        def wrap(*a, **kw):
            ins = fn(*a, **kw)
            self.last = ins
            shim = _InsShim(self, ins)
            if not self._is_pe:
                small = False
                try:
                    out = a[0]
                    fsz = 1
                    for d in out.shape[1:]:
                        fsz *= d
                    small = fsz <= 64
                except Exception:
                    small = False
                if (name in _FENCE_ALWAYS or small
                        or kw.get("accum_out") is not None):
                    self._fc += 1
                    ins.then_inc(self._fs, 1)
                    self._eng.wait_ge(self._fs, self._fc)
                    self._marked[id(ins)] = ins
            return shim
        return wrap

    def fence(self):
        if self.last is not None and not self._is_pe:
            if id(self.last) in self._marked:
                return
            self._fc += 1
            self.last.then_inc(self._fs, 1)
            self._eng.wait_ge(self._fs, self._fc)
            self._marked[id(self.last)] = self.last


def _ceil(a, b):
    return -(-a // b)


class KB:
    def __init__(self):
        self.nc = bass.Bass("TRN2", target_bir_lowering=False)
        self.E = None
        self.eng = None
        self.stage_no = 0
        self.c = {}

    def on(self, name):
        return self.E == name

    def w(self, engname, sem, val=None):
        if self.E == engname:
            self.eng.wait_ge(sem, val if val is not None else self.c[sem.name])

    def inc(self, sem, n):
        self.c[sem.name] += n
        return self.c[sem.name]

    def dma(self, qname, out, in_, sem=None, **kw):
        s = sem if sem is not None else (self.gd if qname == "gpsimd"
                                         else self.ld)
        if self.E == qname:
            self.eng.dma_start(out=out, in_=in_, **kw).then_inc(s, 16)
        self.inc(s, 16)

    def real_nop(self):
        e = self.eng._eng if hasattr(self.eng, "_eng") else self.eng
        if self.E in ("vector", "gpsimd"):
            return e.engine_nop()
        if self.E == "scalar":
            return e.memzero(self.sb_dummy[0:1, 0:1])
        return e.nop()

    def mark(self, sem, n=1):
        ins = getattr(self.eng, "last", None)
        marked = getattr(self.eng, "_marked", {})
        if ins is None or id(ins) in marked:
            ins = self.real_nop()
        else:
            marked[id(ins)] = ins
        ins.then_inc(sem, n)

    def fence(self):
        if hasattr(self.eng, "fence"):
            self.eng.fence()

    def barrier(self):
        self.stage_no += 1
        if self.E == "sync":
            self.eng.wait_ge(self.ld, self.c[self.ld.name])
        if self.E == "gpsimd":
            self.eng.wait_ge(self.gd, self.c[self.gd.name])
        self.mark(self.bar, 1)
        self.eng.wait_ge(self.bar, 5 * self.stage_no)
        if hasattr(self.eng, "last"):
            self.eng.last = None

    def av(self, off, p, cols, dtype=F32):
        v = self.arena[0:p, off:off + (cols if dtype == F32 else _ceil(cols, 2))]
        if dtype == BF16:
            v = v.bitcast(BF16)[:, 0:cols]
        return v

    def declare(self):
        nc = self.nc
        D = lambda n, s, dt, **kw: nc.dram_tensor(n, s, dt, **kw).ap()
        EI = dict(kind="ExternalInput")
        self.i_u = D("u_main", [SEGH, EMBED], F32, **EI)
        self.i_uh = D("u_hist", [SEGH, EMBED], F32, **EI)
        self.i_win = D("w_in_t", [EMBED, E_PAD], F32, **EI)
        self.i_convw = D("conv_w_r", [128, 132], F32, **EI)
        self.i_convb = D("conv_b_r", [128, 33], F32, **EI)
        self.i_alog = D("a_log", [32, 1], F32, **EI)
        self.i_dtb = D("dt_bias", [32, 1], F32, **EI)
        self.i_normw = D("norm_w_r", [128, 32], F32, **EI)
        self.i_dskip = D("dskip_row", [1, 32], F32, **EI)
        self.i_wo = D("w_o_t", [D_INNER, EMBED], F32, **EI)
        self.i_wg = D("wg_blk", [64 * EMBED, 128], F32, **EI)
        self.i_wu = D("wu_blk", [64 * EMBED, 128], F32, **EI)
        self.i_wd = D("w_d_t", [HID, EMBED], F32, **EI)
        self.i_consts = D("consts", [128, 768], F32, **EI)
        self.i_maskc = D("maskc", [1, 8], F32, **EI)
        self.i_ctl = D("ctl", [1, 4], I32, **EI)

        self.o_out = D("out", [SEG, EMBED], F32, kind="ExternalOutput")

        self.d_xn1 = D("d_xn1", [SEGH, EMBED], BF16)
        self.d_xn1h = D("d_xn1h", [SEGH, EMBED], BF16)
        self.d_xc = D("d_xc", [D_INNER, SEG], BF16)
        self.d_xch = D("d_xch", [D_INNER, SEG], BF16)
        self.d_sigz = D("d_sigz", [D_INNER, SEG], BF16)
        self.d_states = D("d_states", [NCH * DSTATE, D_INNER], BF16)
        self.d_yn = D("d_yn", [SEG, D_INNER], BF16)
        self.d_attnfm = D("d_attnfm", [EMBED, SEG], BF16)
        self.d_xn2 = D("d_xn2", [SEG, EMBED], BF16)
        self.d_gup = D("d_gup", [HID, SEG], BF16)
        self.d_tmp = D("d_tmp", [16, 64], F32)
        self.d_bcb = D("d_bcb", [64, SEG], BF16)

        SH = dict(addr_space="Shared")
        self.sh_s1h = D("sh_s1h", [2, DSTATE, D_INNER], F32, **SH)
        self.sh_s1o = D("sh_s1o", [2, DSTATE, D_INNER], F32, **SH)
        self.sh_s2 = D("sh_s2", [2, DSTATE, D_INNER], F32, **SH)
        self.sh_meta = D("sh_meta", [2, 64], F32, **SH)
        self.sh_flags = D("sh_flags", [1, 4], I32, **SH)

        self.o_dbg = {}
        for name, spec in DEBUG_TAPS:
            self.o_dbg[name] = D("dbg_" + name, spec["shape"],
                                 {"f32": F32, "bf16": BF16}[spec.get("dtype", "f32")],
                                 kind="ExternalOutput")

    def alloc(self, es):
        nc = self.nc
        S = lambda n, s, dt: es.enter_context(nc.sbuf_tensor(n, s, dt))
        self.sb_consts = S("sbconsts", [128, 768], F32)
        self.sb_consts_bf = S("consts_bf", [128, 768], BF16)
        self.sb_convw = S("convw", [128, 132], F32)
        self.sb_convb = S("convb", [128, 33], F32)
        self.sb_alog = S("alog", [32, 1], F32)
        self.sb_dtb = S("dtb", [32, 1], F32)
        self.sb_A = S("Acol", [32, 1], F32)
        self.sb_normw = S("normw", [128, 32], F32)
        self.sb_maskc = S("sbmaskc", [1, 8], F32)
        self.sb_maskb = S("maskb", [64, 8], F32)
        self.sb_ctl = S("sbctl", [1, 4], I32)
        self.sb_pollf = S("pollf", [1, 4], I32)
        self.sb_biasc = S("biasc", [128, 3], F32)
        self.sb_dummy = S("sbdummy", [1, 2], F32)
        self.sb_dskipI = S("dskipI", [128, 32 * 128], BF16)
        self.sb_row = S("rowbuf", [1, 128], F32)
        self.sb_row2 = S("rowbuf2", [1, 128], F32)
        self.sb_dtsp_tm = S("dtsp_tm", [128, 256], F32)
        self.sb_dA_tm = S("dA_tm", [128, 256], F32)
        self.sb_expa_tm = S("expa_tm", [128, 256], F32)
        self.sb_dtdec_tm = S("dtdec_tm", [128, 256], F32)
        self.sb_dtraw = S("dtraw", [32, SEGH], F32)
        self.sb_dtsp = S("dtsp_cm", [32, SEG], F32)
        self.sb_dA = S("dA_cm", [32, SEG], F32)
        self.sb_acm = S("a_cm", [32, SEG], F32)
        self.sb_z32 = S("z32", [32, 256], F32)
        self.sb_alast = S("alast", [32, 4], F32)
        self.sb_logd = S("logd", [32, 2], F32)
        self.sb_decrep = S("decrep", [64, 128], F32)
        self.sb_decown = S("decown", [64, 32], F32)
        self.sb_BC = S("BC", [128, SEG], F32)
        self.sb_BC_bf = S("BC_bf", [128, SEG], BF16)
        self.sb_Ct0 = S("Ct0", [64, SEG], BF16)
        self.sb_Btm = S("Btm", [128, 8 * 64], BF16)
        self.sb_scoresm = S("scoresm", [128, 4 * 384], BF16)
        self.sb_srun = S("srun", [64, D_INNER], F32)
        self.sb_pairmeta = S("pairmeta", [1, 64], F32)
        self.sb_decpair = S("decpair", [64, 64], F32)
        self.arena = S("arena", [128, ARENA], F32)
        self.sb_sEh = self.av(18800, 64, D_INNER)
        self.sb_sEo = self.av(22900, 64, D_INNER)
        self.sb_pairbuf = self.av(27000, 64, D_INNER)
        self.sb_t32a = self.av(31100, 32, SEG)
        self.sb_t32b = self.av(32130, 32, SEG)
        self.pb = [es.enter_context(nc.psum_tensor(f"pb{i}", [128, 512], F32))
                   for i in range(8)]
        sem = lambda n: es.enter_context(nc.semaphore(n))
        self.ld = sem("ld")
        self.gd = sem("gd")
        self.pe = sem("pe")
        self.ve = sem("ve")
        self.ac = sem("ac")
        self.po = sem("po")
        self.bar = sem("bar")
        self.pol = sem("pol")
        self.f_ve = sem("f_ve")
        self.f_ac = sem("f_ac")
        self.f_po = sem("f_po")
        self.f_sy = sem("f_sy")
        self.wsA = sem("wsA")
        self.wsB = sem("wsB")
        self.gbA = sem("gbA")
        self.gbB = sem("gbB")
        for s in (self.ld, self.gd, self.pe, self.ve, self.ac, self.po,
                  self.bar, self.pol, self.wsA, self.wsB, self.gbA, self.gbB):
            self.c[s.name] = 0

    def s_setup(self):
        for dst, src in [(self.sb_consts[:], self.i_consts),
                         (self.sb_convw[:], self.i_convw),
                         (self.sb_convb[:], self.i_convb),
                         (self.sb_alog[:], self.i_alog),
                         (self.sb_dtb[:], self.i_dtb),
                         (self.sb_normw[:], self.i_normw),
                         (self.sb_maskc[:], self.i_maskc),
                         (self.sb_ctl[:], self.i_ctl),
                         (self.sb_row[0:1, 0:32], self.i_dskip)]:
            self.dma("sync", dst, src)
        self.w("vector", self.ld)
        self.w("scalar", self.ld)
        self.w("tensor", self.ld)
        if self.on("vector"):
            self.eng.tensor_copy(self.sb_consts_bf[:], self.sb_consts[:])
            self.eng.memset(self.sb_z32[:], 0.0)
            self.eng.memset(self.sb_biasc[:, 0:1], EPS_LN)
            self.eng.memset(self.sb_biasc[:, 1:2], EPS_RMS)
            self.eng.memset(self.sb_biasc[:, 2:3], 1.0)
        if self.on("scalar"):
            self.eng.activation(self.sb_A[:], self.sb_alog[:], AF.Exp)
            self.eng.mul(self.sb_A[:], self.sb_A[:], -1.0).then_inc(self.ac, 1)
        self.inc(self.ac, 1)
        if self.on("tensor"):
            self.eng.matmul(self.pb[0][0:128, 0:32],
                            self.sb_consts[0:1, C_ON:C_ON + 128],
                            self.sb_row[0:1, 0:32], start=True, stop=True,
                            skip_group_check=True)
            self.eng.matmul(self.pb[1][0:64, 0:8],
                            self.sb_consts[0:1, C_ON:C_ON + 64],
                            self.sb_maskc[0:1, 0:8], start=True, stop=True,
                            skip_group_check=True)
            self.mark(self.pe)
        self.inc(self.pe, 1)
        self.w("vector", self.pe)
        dskip_b = self.av(0, 128, 32)
        if self.on("vector"):
            self.eng.tensor_copy(dskip_b, self.pb[0][0:128, 0:32])
            self.eng.tensor_copy(self.sb_maskb[:], self.pb[1][0:64, 0:8])
            for h in range(NHEADS):
                self.eng.tensor_scalar(
                    self.sb_dskipI[:, h * 128:(h + 1) * 128],
                    self.sb_consts[:, C_ID:C_ID + 128],
                    dskip_b[:, h:h + 1], None, ALU.mult)
            self.mark(self.ve)
        self.inc(self.ve, 1)
        self.barrier()

    def s_ln1(self, hist):
        src = self.i_uh if hist else self.i_u
        dst = self.d_xn1h if hist else self.d_xn1
        A = 40
        xt = self.av(A, 128, 2048)
        sq = self.av(A + 2050, 128, 2048)
        stats = self.av(A + 4100, 128, 28)
        xn = self.av(A + 4130, 128, 2048, BF16)
        for t in range(9):
            self.dma("sync", xt, src[t * 128:(t + 1) * 128, :])
            self.w("vector", self.ld)
            self.w("scalar", self.ld)
            self._ln_small(xt, sq, stats, 2048, EPS_LN)
            if self.on("vector"):
                self.eng.tensor_scalar(xn, xt, stats[:, 4:5], stats[:, 24:25],
                                       ALU.subtract, ALU.mult).then_inc(self.ve, 1)
            self.inc(self.ve, 1)
            self.w("sync", self.ve)
            self.dma("sync", dst[t * 128:(t + 1) * 128, :], xn)
        self.barrier()

    def _ln_small(self, xt, sq, stats, Dn, eps):
        C = lambda i: stats[:, 4 * i:4 * i + 1]
        if self.on("vector"):
            self.eng.reduce_sum(C(0), xt, axis=AX)
            self.eng.tensor_scalar(C(1), C(0), 1.0 / Dn,
                                   None, ALU.mult).then_inc(self.ve, 1)
        self.inc(self.ve, 1)
        if self.on("scalar"):
            self.eng.activation(sq, xt, AF.Square,
                                accum_out=C(2)).then_inc(self.ac, 1)
        self.inc(self.ac, 1)
        self.w("vector", self.ac)
        if self.on("vector"):
            self.eng.tensor_scalar(C(3), C(2), 1.0 / Dn, None, ALU.mult)
            self.eng.tensor_tensor(C(4), C(1), C(1), ALU.mult)
            self.eng.tensor_sub(C(3), C(3), C(4))
            self.mark(self.ve)
        self.inc(self.ve, 1)
        self.w("scalar", self.ve)
        if self.on("scalar"):
            self.eng.activation(C(5), C(3), AF.Sqrt,
                                bias=self.sb_biasc[:, 0:1]).then_inc(self.ac, 1)
        self.inc(self.ac, 1)
        self.w("vector", self.ac)
        if self.on("vector"):
            self.eng.reciprocal(C(6), C(5))
            self.mark(self.ve)
        self.inc(self.ve, 1)

    def s_mixer_front(self, hist):
        xn1_d = self.d_xn1h if hist else self.d_xn1
        xc_d = self.d_xch if hist else self.d_xc
        A = 40
        xn1T = [self.av(A + k * 576, 128, SEGH, BF16) for k in range(NK)]
        A2 = A + NK * 576
        wg_buf = [self.av(A2 + i * 3072, 128, 16 * 384, BF16) for i in range(2)]
        A3 = A2 + 6144
        xbc = self.av(A3, 128, SEGH)
        convout = self.av(A3 + 1160, 128, SEG)
        xcbf = self.av(A3 + 2190, 128, SEG, BF16)
        sigbf = self.av(A3 + 2710, 128, SEGH, BF16)

        for k in range(NK):
            self.dma("sync", xn1T[k], xn1_d[:, k * 128:(k + 1) * 128],
                     transpose=True)
        if MF_SUB == "xn1t":
            self.w("sync", self.ld)
            self.barrier()
            return
        e_lo = NE_XBC0 if hist else 0
        tiles = list(range(e_lo, NE))
        groups = [tiles[i:i + 3] for i in range(0, len(tiles), 3)]
        NCHK = ((0, 512), (512, 1024), (1024, 1152))
        self.w("tensor", self.ld)
        self.w("gpsimd", self.ld)
        if MF_SUB == "w":
            for gi, grp in enumerate(groups[:4]):
                wb = wg_buf[gi % 2]
                e0 = grp[0]
                ncols = len(grp) * 128
                for kq in range(4):
                    src = self.i_win[kq * 512:(kq + 1) * 512,
                                     e0 * 128:e0 * 128 + ncols]
                    src = src.rearrange("(q p) e -> p q e", p=128)
                    dstv = wb[:, kq * 4 * 384: kq * 4 * 384 + 4 * ncols]
                    dstv = dstv.rearrange("p (q e) -> p q e", q=4)
                    self.dma("gpsimd", dstv, src)
                self.w("gpsimd", self.gd)
            self.barrier()
            return
        if MF_SUB and MF_SUB.startswith("mm"):
            groups = groups[:int(MF_SUB[2:])]
        grp_pe = []
        grp_done = []
        for gi, grp in enumerate(groups):
            wb = wg_buf[gi % 2]
            if gi >= 2:
                self.w("gpsimd", self.pe, grp_pe[gi - 2])
            if "gwdly" in MF_FLAGS and gi >= 1:
                self.w("gpsimd", self.ve, grp_done[gi - 1][0])
                self.w("gpsimd", self.ac, grp_done[gi - 1][1])
            e0 = grp[0]
            ncols = len(grp) * 128
            wsem = self.wsA if gi % 2 == 0 else self.wsB
            for kq in range(4):
                src = self.i_win[kq * 512:(kq + 1) * 512,
                                 e0 * 128:e0 * 128 + ncols]
                src = src.rearrange("(q p) e -> p q e", p=128)
                dstv = wb[:, kq * 4 * 384: (kq + 1) * 4 * 384]
                dstv = dstv.rearrange("p (q e) -> p q e", e=384)[:, :, 0:ncols]
                self.dma("gpsimd", dstv, src, sem=wsem)
            self.w("tensor", wsem)
            if gi >= 1:
                self.w("tensor", self.ve, grp_done[gi - 1][0])
                self.w("tensor", self.ac, grp_done[gi - 1][1])
                if "gser" in MF_FLAGS:
                    self.w("tensor", self.ld)
                    self.w("tensor", self.gd)
            for el, e in enumerate(grp):
                for k in range(NK):
                    lhsT = wb[:, (k * 384) + el * 128: (k * 384) + el * 128 + 128]
                    for ci, (c0, c1) in enumerate(NCHK):
                        if ci < 2:
                            tgt = self.pb[2 * el + ci][:, 0:512]
                        elif el < 2:
                            tgt = self.pb[6][:, el * 128:el * 128 + 128]
                        else:
                            tgt = self.pb[7][:, 0:128]
                        if self.on("tensor"):
                            self.eng.matmul(tgt, lhsT, xn1T[k][:, c0:c1],
                                            start=(k == 0), stop=(k == NK - 1),
                                            skip_group_check=True)
            if self.on("tensor"):
                self.mark(self.pe)
            grp_pe.append(self.inc(self.pe, 1))
            self.w("vector", self.pe)
            self.w("scalar", self.pe)
            for el, e in enumerate(grp):
                if "noevac" in MF_FLAGS:
                    continue
                if "vecopyonly" in MF_FLAGS:
                    psl = [self.pb[2 * el][:, 0:512],
                           self.pb[2 * el + 1][:, 0:512],
                           self.pb[6][:, el * 128:el * 128 + 128] if el < 2
                           else self.pb[7][:, 0:128]]
                    if self.on("vector"):
                        self.eng.tensor_copy(xbc[:, 0:512], psl[0])
                        self.eng.tensor_copy(xbc[:, 512:1024], psl[1])
                        self.eng.tensor_copy(xbc[:, 1024:1152], psl[2])
                        self.mark(self.ve)
                    self.inc(self.ve, 1)
                    continue
                psl = [self.pb[2 * el][:, 0:512], self.pb[2 * el + 1][:, 0:512],
                       self.pb[6][:, el * 128:el * 128 + 128] if el < 2
                       else self.pb[7][:, 0:128]]
                if "fakeevac" in MF_FLAGS:
                    if self.on("vector"):
                        cs = self.sb_consts[:, 0:512]
                        self.eng.tensor_copy(xbc[:, 0:512], cs)
                        self.eng.tensor_copy(xbc[:, 512:1024], cs)
                        self.eng.tensor_copy(xbc[:, 1024:1152], cs[:, 0:128])
                        self.mark(self.ve)
                    self.inc(self.ve, 1)
                    continue
                if e < NE_XBC0:
                    self.w("scalar", self.ld)
                    zf = AF.Copy if "zcopy" in MF_FLAGS else AF.Sigmoid
                    if self.on("scalar"):
                        self.eng.activation(sigbf[:, 0:512], psl[0], zf)
                        self.eng.activation(sigbf[:, 512:1024], psl[1], zf)
                        self.eng.activation(sigbf[:, 1024:1152], psl[2], zf)
                        self.mark(self.ac)
                    self.inc(self.ac, 1)
                    self.w("sync", self.ac)
                    self.dma("sync", self.d_sigz[e * 128:(e + 1) * 128, :],
                             sigbf[:, 128:1152])
                elif e < NE_XBC1:
                    j = e - NE_XBC0
                    self.w("vector", self.ac)
                    if self.on("vector"):
                        self.eng.tensor_copy(xbc[:, 0:512], psl[0])
                        self.eng.tensor_copy(xbc[:, 512:1024], psl[1])
                        self.eng.tensor_copy(xbc[:, 1024:1152], psl[2])
                        self.fence()
                        if "noconv" not in MF_FLAGS:
                            wk = [self.sb_convw[:, 4 * j + kk: 4 * j + kk + 1]
                                  for kk in range(4)]
                            self.eng.tensor_scalar(convout, xbc[:, 125:1149],
                                                   wk[0], None, ALU.mult)
                            self.eng.scalar_tensor_tensor(convout,
                                                          xbc[:, 126:1150],
                                                          wk[1], convout,
                                                          ALU.mult, ALU.add)
                            self.eng.scalar_tensor_tensor(convout,
                                                          xbc[:, 127:1151],
                                                          wk[2], convout,
                                                          ALU.mult, ALU.add)
                            self.eng.scalar_tensor_tensor(convout,
                                                          xbc[:, 128:1152],
                                                          wk[3], convout,
                                                          ALU.mult, ALU.add)
                        self.mark(self.ve)
                    self.inc(self.ve, 1)
                    self.w("scalar", self.ve)
                    if j == 32:
                        if self.on("scalar"):
                            self.eng.activation(self.sb_BC[:], convout, AF.Silu,
                                                bias=self.sb_convb[:, j:j + 1])
                            self.mark(self.ac)
                        self.inc(self.ac, 1)
                        self.w("vector", self.ac)
                        if self.on("vector"):
                            self.eng.tensor_copy(self.sb_BC_bf[:], self.sb_BC[:])
                            self.mark(self.ve)
                        self.inc(self.ve, 1)
                    else:
                        self.w("scalar", self.ld)
                        if self.on("scalar"):
                            if "nosilu" in MF_FLAGS:
                                self.eng.activation(xcbf, convout, AF.Copy)
                            else:
                                self.eng.activation(
                                    xcbf, convout, AF.Silu,
                                    bias=self.sb_convb[:, j:j + 1])
                            self.mark(self.ac)
                        self.inc(self.ac, 1)
                        if "nodma" not in MF_FLAGS:
                            self.w("sync", self.ac)
                            self.dma("sync", xc_d[j * 128:(j + 1) * 128, :],
                                     xcbf)
                else:
                    if self.on("vector"):
                        self.eng.tensor_copy(self.sb_dtraw[:, 0:512],
                                             psl[0][0:32, :])
                        self.eng.tensor_copy(self.sb_dtraw[:, 512:1024],
                                             psl[1][0:32, :])
                        self.eng.tensor_copy(self.sb_dtraw[:, 1024:1152],
                                             psl[2][0:32, :])
                        self.mark(self.ve)
                    self.inc(self.ve, 1)
            if self.on("vector"):
                self.mark(self.ve)
            v1 = self.inc(self.ve, 1)
            if self.on("scalar"):
                self.mark(self.ac)
            a1 = self.inc(self.ac, 1)
            grp_done.append((v1, a1))
            if "gbar" in MF_FLAGS:
                self.barrier()
        self.barrier()
        if MF_PARTS < 2:
            return

        if not hist:
            self.dma("sync", self.d_bcb, self.sb_BC_bf[64:128, :])
            self.w("sync", self.ld)
            self.dma("sync", self.sb_Ct0[:], self.d_bcb)

        dcol = 0 if hist else 1
        t32a, t32b = self.sb_t32a, self.sb_t32b
        dtr = self.sb_dtraw[:, 128:1152]
        if self.on("scalar"):
            self.eng.activation(t32a[:], dtr, AF.Abs, bias=self.sb_dtb[:])
            self.eng.activation(t32a[:], t32a[:], AF.Exp, scale=-1.0)
            self.eng.activation(t32a[:], t32a[:], AF.Ln,
                                bias=self.sb_biasc[0:32, 2:3])
            self.eng.activation(t32b[:], dtr, AF.Relu, bias=self.sb_dtb[:])
            self.mark(self.ac)
        self.inc(self.ac, 1)
        self.w("vector", self.ac)
        if self.on("vector"):
            self.eng.tensor_add(self.sb_dtsp[:], t32a[:], t32b[:])
            self.fence()
            self.eng.tensor_scalar(self.sb_dA[:], self.sb_dtsp[:], self.sb_A[:],
                                   None, ALU.mult)
            self.fence()
            for cc in range(NCH):
                sl = slice(cc * 256, (cc + 1) * 256)
                self.eng.tensor_tensor_scan(self.sb_acm[:, sl], self.sb_dA[:, sl],
                                            self.sb_z32[:], 0.0, ALU.add, ALU.add)
                self.eng.tensor_copy(self.sb_alast[:, cc:cc + 1],
                                     self.sb_acm[:, cc * 256 + 255:cc * 256 + 256])
            self.eng.tensor_add(self.sb_logd[:, dcol:dcol + 1],
                                self.sb_alast[:, 0:1], self.sb_alast[:, 1:2])
            self.eng.tensor_add(self.sb_logd[:, dcol:dcol + 1],
                                self.sb_logd[:, dcol:dcol + 1],
                                self.sb_alast[:, 2:3])
            self.eng.tensor_add(self.sb_logd[:, dcol:dcol + 1],
                                self.sb_logd[:, dcol:dcol + 1],
                                self.sb_alast[:, 3:4])
            for cc in range(NCH):
                sl = slice(cc * 256, (cc + 1) * 256)
                self.eng.tensor_scalar(t32a[:, sl], self.sb_acm[:, sl],
                                       self.sb_alast[:, cc:cc + 1], None,
                                       ALU.subtract)
            self.mark(self.ve)
        self.inc(self.ve, 1)
        self.w("scalar", self.ve)
        if self.on("scalar"):
            self.eng.activation(t32b[:], t32a[:], AF.Exp, scale=-1.0)
            self.mark(self.ac)
        self.inc(self.ac, 1)
        self.w("vector", self.ac)
        if self.on("vector"):
            self.eng.tensor_mul(t32a[:], self.sb_dtsp[:], t32b[:])
            self.mark(self.ve)
        self.inc(self.ve, 1)
        self.w("sync", self.ve)
        for cc in range(NCH):
            self.dma("sync", self.d_tmp[cc:cc + 1, 0:32],
                     self.sb_alast[:, cc:cc + 1])
        self.dma("sync", self.d_tmp[4 + dcol:5 + dcol, 0:32],
                 self.sb_logd[:, dcol:dcol + 1])
        self.w("sync", self.ld)
        for cc in range(NCH):
            self.dma("sync", self.sb_row[0:1, cc * 32:(cc + 1) * 32],
                     self.d_tmp[cc:cc + 1, 0:32])
        self.w("scalar", self.ld)
        if self.on("scalar"):
            self.eng.activation(self.sb_row2[0:1, 0:128], self.sb_row[0:1, 0:128],
                                AF.Exp).then_inc(self.ac, 1)
        self.inc(self.ac, 1)
        self.w("tensor", self.ac)
        for cc in range(NCH):
            if self.on("tensor"):
                self.eng.matmul(self.pb[7][0:64, cc * 32:(cc + 1) * 32],
                                self.sb_consts[0:1, C_ON:C_ON + 64],
                                self.sb_row2[0:1, cc * 32:(cc + 1) * 32],
                                start=True, stop=True, skip_group_check=True)
        if self.on("tensor"):
            self.mark(self.pe)
        self.inc(self.pe, 1)
        self.w("vector", self.pe)
        if self.on("vector"):
            self.eng.tensor_copy(self.sb_decrep[:, 0:128], self.pb[7][0:64, 0:128])
            self.mark(self.ve)
        self.inc(self.ve, 1)
        self.w("tensor", self.ve)
        for tt in range(8):
            src_list = [(self.sb_dA, self.sb_dA_tm),
                        (self.sb_dtsp, self.sb_dtsp_tm),
                        (self.sb_acm, self.sb_expa_tm),
                        (t32a, self.sb_dtdec_tm)]
            for si, (src, dst) in enumerate(src_list):
                if self.on("tensor"):
                    self.eng.transpose(self.pb[si][0:128, 0:32],
                                       src[:, tt * 128:(tt + 1) * 128],
                                       self.sb_consts[0:32, C_ID:C_ID + 32])
                    self.mark(self.pe)
                self.inc(self.pe, 1)
                self.w("vector", self.pe)
                if self.on("vector"):
                    self.eng.tensor_copy(dst[:, tt * 32:(tt + 1) * 32],
                                         self.pb[si][0:128, 0:32])
                    self.mark(self.ve)
                self.inc(self.ve, 1)
                self.w("tensor", self.ve)
        self.w("scalar", self.ve)
        if self.on("scalar"):
            self.eng.activation(self.sb_expa_tm[:], self.sb_expa_tm[:], AF.Exp)
            self.mark(self.ac)
        self.inc(self.ac, 1)
        self.barrier()
        if MF_PARTS < 3:
            return

        sE = self.sb_sEh if hist else self.sb_sEo
        AS = 40
        xctm = [self.av(AS + i * 2048, 128, 4096, BF16) for i in range(2)]
        xdd = [self.av(AS + 4096 + i * 2048, 128, 4096, BF16) for i in range(2)]
        st_f = self.av(AS + 8200, 64, 4096)
        st_bf = self.av(AS + 12300, 64, 4096, BF16)
        for cc in range(NCH):
            for qt in range(2):
                if self.on("tensor"):
                    self.eng.transpose(
                        self.pb[7].bitcast(BF16)[0:128, qt * 64:(qt + 1) * 64],
                        self.sb_BC_bf[0:64,
                                      cc * 256 + qt * 128:cc * 256 + (qt + 1) * 128],
                        self.sb_consts_bf[0:64, C_ID:C_ID + 64])
            if self.on("tensor"):
                self.mark(self.pe)
            self.inc(self.pe, 1)
            self.w("vector", self.pe)
            if self.on("vector"):
                self.eng.tensor_copy(self.sb_Btm[:, cc * 128:cc * 128 + 128],
                                     self.pb[7].bitcast(BF16)[0:128, 0:128])
                self.mark(self.ve)
            self.inc(self.ve, 1)
            for qt in range(2):
                self.dma("sync", xctm[qt],
                         xc_d[:, cc * 256 + qt * 128:cc * 256 + (qt + 1) * 128],
                         transpose=True)
            self.w("vector", self.ld)
            for qt in range(2):
                tt = cc * 2 + qt
                dv = self.sb_dtdec_tm[:, tt * 32:(tt + 1) * 32]
                dv = dv.rearrange("p (h o) -> p h o", o=1).broadcast_to((128, 32, 128))
                if self.on("vector"):
                    self.eng.tensor_tensor(
                        xdd[qt].rearrange("p (h q) -> p h q", h=32),
                        xctm[qt].rearrange("p (h q) -> p h q", h=32),
                        dv, ALU.mult)
            if self.on("vector"):
                self.mark(self.ve)
            self.inc(self.ve, 1)
            self.w("tensor", self.ve)
            for qt in range(2):
                for nchk in range(8):
                    if self.on("tensor"):
                        self.eng.matmul(
                            self.pb[nchk][0:64, 0:512],
                            self.sb_Btm[:, cc * 128 + qt * 64:
                                        cc * 128 + (qt + 1) * 64],
                            xdd[qt][:, nchk * 512:(nchk + 1) * 512],
                            start=(qt == 0), stop=(qt == 1),
                            skip_group_check=True)
            if self.on("tensor"):
                self.mark(self.pe)
            self.inc(self.pe, 1)
            self.w("vector", self.pe)
            if self.on("vector"):
                for nchk in range(8):
                    self.eng.tensor_copy(st_f[:, nchk * 512:(nchk + 1) * 512],
                                         self.pb[nchk][0:64, 0:512])
                self.fence()
                if not hist:
                    self.eng.tensor_copy(st_bf, st_f)
                if cc == 0:
                    self.eng.tensor_copy(sE[:], st_f)
                else:
                    for h in range(NHEADS):
                        hl = slice(h * 128, (h + 1) * 128)
                        self.eng.scalar_tensor_tensor(
                            sE[:, hl], sE[:, hl],
                            self.sb_decrep[:, cc * 32 + h:cc * 32 + h + 1],
                            st_f[:, hl], ALU.mult, ALU.add)
                self.mark(self.ve)
            self.inc(self.ve, 1)
            if not hist:
                self.w("sync", self.ve)
                self.dma("sync", self.d_states[cc * 64:(cc + 1) * 64, :], st_bf)
            self.w("tensor", self.ve)
            if not hist:
                self.w("tensor", self.ld)
                if self.on("tensor"):
                    csl = slice(cc * 256, (cc + 1) * 256)
                    self.eng.matmul(self.pb[7][:, 0:256],
                                    self.sb_BC_bf[0:64, cc * 256:cc * 256 + 128],
                                    self.sb_Ct0[0:64, csl],
                                    start=True, stop=True, skip_group_check=True)
                    self.eng.matmul(
                        self.pb[7][:, 256:384],
                        self.sb_BC_bf[0:64, cc * 256 + 128:cc * 256 + 256],
                        self.sb_Ct0[0:64, cc * 256 + 128:cc * 256 + 256],
                        start=True, stop=True, skip_group_check=True)
                    self.mark(self.pe)
                self.inc(self.pe, 1)
                self.w("vector", self.pe)
                if self.on("vector"):
                    sm = self.sb_scoresm[:, cc * 384:(cc + 1) * 384]
                    self.eng.tensor_tensor(sm[:, 0:128], self.pb[7][:, 0:128],
                                           self.sb_consts[:, C_UT:C_UT + 128],
                                           ALU.mult)
                    self.eng.tensor_tensor(sm[:, 128:256], self.pb[7][:, 128:256],
                                           self.sb_consts[:, C_ON:C_ON + 128],
                                           ALU.mult)
                    self.eng.tensor_tensor(sm[:, 256:384], self.pb[7][:, 256:384],
                                           self.sb_consts[:, C_UT:C_UT + 128],
                                           ALU.mult)
                    self.mark(self.ve)
                self.inc(self.ve, 1)
                self.w("tensor", self.ve)
        self.barrier()

    def s_exchange(self):
        ve0 = self.c[self.ve.name]
        po0 = self.c[self.po.name]
        if self.on("gpsimd"):
            g = self.eng
            with contextlib.ExitStack() as rs:
                r_slot = rs.enter_context(g.register("r_slot"))
                r_nc1 = rs.enter_context(g.register("r_nc1"))
                r_nc2 = rs.enter_context(g.register("r_nc2"))
                r_exp = rs.enter_context(g.register("r_exp"))
                r_f0 = rs.enter_context(g.register("r_f0"))
                r_f1 = rs.enter_context(g.register("r_f1"))
                r_go = rs.enter_context(g.register("r_go"))
                g.reg_load(r_slot, self.sb_ctl[0:1, 0:1])
                g.reg_load(r_nc1, self.sb_ctl[0:1, 1:2])
                g.reg_load(r_nc2, self.sb_ctl[0:1, 2:3])
                n = [self.c[self.gd.name]]

                def send(dst0, dst1, srcv):
                    with g.If_eq(r_slot, 0):
                        g.dma_start(out=dst0, in_=srcv).then_inc(self.gd, 16)
                    with g.Else():
                        g.dma_start(out=dst1, in_=srcv).then_inc(self.gd, 16)
                    n[0] += 16

                def recv(dstv, src0, src1):
                    with g.If_eq(r_slot, 0):
                        g.dma_start(out=dstv, in_=src1).then_inc(self.gd, 16)
                    with g.Else():
                        g.dma_start(out=dstv, in_=src0).then_inc(self.gd, 16)
                    n[0] += 16

                def poll(col0, col1, r_nc):
                    g.reg_mov(r_exp, 16)
                    g.reg_mov(r_go, 1)
                    with g.While(r_go):
                        g.dma_start(out=self.sb_pollf[:],
                                    in_=self.sh_flags).then_inc(self.pol, 16)
                        g.wait_ge(self.pol, r_exp)
                        g.reg_add(r_exp, r_exp, 16)
                        g.reg_load(r_f0, self.sb_pollf[0:1, col0:col0 + 1])
                        g.reg_load(r_f1, self.sb_pollf[0:1, col1:col1 + 1])
                        g.reg_alu(r_f0, r_f0, r_nc, ALU.is_equal)
                        g.reg_alu(r_f1, r_f1, r_nc, ALU.is_equal)
                        g.reg_alu(r_go, r_f0, r_f1, ALU.bitwise_and)
                        g.reg_alu(r_go, r_go, 1, ALU.bitwise_xor)

                send(self.sh_s1h[0], self.sh_s1h[1], self.sb_sEh[:])
                send(self.sh_s1o[0], self.sh_s1o[1], self.sb_sEo[:])
                send(self.sh_meta[0:1, 0:32], self.sh_meta[1:2, 0:32],
                     self.d_tmp[4:5, 0:32])
                send(self.sh_meta[0:1, 32:64], self.sh_meta[1:2, 32:64],
                     self.d_tmp[5:6, 0:32])
                g.wait_ge(self.gd, n[0])
                send(self.sh_flags[0:1, 0:1], self.sh_flags[0:1, 1:2],
                     self.sb_ctl[0:1, 1:2])
                g.wait_ge(self.gd, n[0])
                poll(0, 1, r_nc1)
                recv(self.sb_pairmeta[0:1, 0:64], self.sh_meta[0:1, :],
                     self.sh_meta[1:2, :])
                recv(self.sb_pairbuf[:], self.sh_s1h[0], self.sh_s1h[1])
                g.wait_ge(self.gd, n[0])
                g.nop().then_inc(self.po, 1)
                g.wait_ge(self.ve, ve0 + 1)
                recv(self.sb_pairbuf[:], self.sh_s1o[0], self.sh_s1o[1])
                g.wait_ge(self.gd, n[0])
                g.nop().then_inc(self.po, 1)
                g.wait_ge(self.ve, ve0 + 2)
                send(self.sh_s2[0], self.sh_s2[1], self.sb_srun[:])
                g.wait_ge(self.gd, n[0])
                send(self.sh_flags[0:1, 2:3], self.sh_flags[0:1, 3:4],
                     self.sb_ctl[0:1, 2:3])
                g.wait_ge(self.gd, n[0])
                poll(2, 3, r_nc2)
                recv(self.sb_pairbuf[:], self.sh_s2[0], self.sh_s2[1])
                g.wait_ge(self.gd, n[0])
                g.nop().then_inc(self.po, 1)
        self.inc(self.gd, 11 * 16)
        self.inc(self.po, 3)

        self.dma("sync", self.sb_row[0:1, 64:96], self.d_tmp[5:6, 0:32])
        self.w("scalar", self.po, po0 + 1)
        self.w("scalar", self.ld)
        if self.on("scalar"):
            self.eng.activation(self.sb_row2[0:1, 0:64],
                                self.sb_pairmeta[0:1, 0:64], AF.Exp)
            self.eng.activation(self.sb_row2[0:1, 64:96],
                                self.sb_row[0:1, 64:96],
                                AF.Exp).then_inc(self.ac, 1)
        self.inc(self.ac, 1)
        self.w("tensor", self.ac)
        if self.on("tensor"):
            self.eng.matmul(self.pb[0][0:64, 0:64],
                            self.sb_consts[0:1, C_ON:C_ON + 64],
                            self.sb_row2[0:1, 0:64], start=True, stop=True,
                            skip_group_check=True)
            self.eng.matmul(self.pb[1][0:64, 0:32],
                            self.sb_consts[0:1, C_ON:C_ON + 64],
                            self.sb_row2[0:1, 64:96], start=True, stop=True,
                            skip_group_check=True)
            self.mark(self.pe)
        self.inc(self.pe, 1)
        self.w("vector", self.pe)
        self.w("vector", self.po, po0 + 1)
        if self.on("vector"):
            self.eng.tensor_copy(self.sb_decpair[:, 0:64], self.pb[0][0:64, 0:64])
            self.eng.tensor_copy(self.sb_decown[:], self.pb[1][0:64, 0:32])
            for h in range(NHEADS):
                hl = slice(h * 128, (h + 1) * 128)
                self.eng.tensor_scalar(self.sb_srun[:, hl], self.sb_sEh[:, hl],
                                       self.sb_decpair[:, h:h + 1],
                                       self.sb_maskb[:, 0:1], ALU.mult, ALU.mult)
            self.eng.scalar_tensor_tensor(self.sb_srun[:], self.sb_pairbuf[:],
                                          self.sb_maskb[:, 1:2], self.sb_srun[:],
                                          ALU.mult, ALU.add)
            self.mark(self.ve)
        self.inc(self.ve, 1)
        self.w("vector", self.po, po0 + 2)
        if self.on("vector"):
            self.eng.scalar_tensor_tensor(self.sb_srun[:], self.sb_pairbuf[:],
                                          self.sb_maskb[:, 2:3], self.sb_srun[:],
                                          ALU.mult, ALU.add)
            self.eng.tensor_copy(self.sb_sEh[:], self.sb_srun[:])
            for h in range(NHEADS):
                hl = slice(h * 128, (h + 1) * 128)
                self.eng.scalar_tensor_tensor(self.sb_srun[:, hl],
                                              self.sb_srun[:, hl],
                                              self.sb_decown[:, h:h + 1],
                                              self.sb_sEo[:, hl],
                                              ALU.mult, ALU.add)
            self.mark(self.ve)
        self.inc(self.ve, 1)
        self.w("vector", self.po, po0 + 3)
        if self.on("vector"):
            self.eng.scalar_tensor_tensor(self.sb_srun[:], self.sb_pairbuf[:],
                                          self.sb_maskb[:, 3:4], self.sb_sEh[:],
                                          ALU.mult, ALU.add)
            self.mark(self.ve)
        self.inc(self.ve, 1)
        self.barrier()

    def s_phase2(self):
        A = 40
        xctm = [self.av(A + i * 2048, 128, 4096, BF16) for i in range(2)]
        xdt = [self.av(A + 4096 + i * 2048, 128, 4096, BF16) for i in range(2)]
        sigz = [self.av(A + 8192 + i * 2048, 128, 4096, BF16) for i in range(2)]
        st_bf = self.av(A + 12288, 64, 4096, BF16)
        srun_bf = self.av(A + 14340, 64, 4096, BF16)
        yg = [self.av(A + 16390 + i * 4096, 128, 4096) for i in range(2)]
        ysq = self.av(A + 24582, 128, 4096, BF16)
        rss = self.av(A + 26630, 128, 96)
        ynbf = self.av(A + 26730, 128, 4096, BF16)
        band = self.av(A + 28790, 128, 384, BF16)
        wexp = self.av(A + 28990, 128, 384, BF16)
        wsb = self.av(A + 29190, 128, 384, BF16)
        rawsb = self.av(A + 29390, 128, 1024, BF16)
        diagE = self.av(A + 29910, 128, 128, BF16)

        for cc in range(NCH):
            for qt in range(2):
                self.dma("sync", xctm[qt],
                         self.d_xc[:, cc * 256 + qt * 128:cc * 256 + (qt + 1) * 128],
                         transpose=True)
                self.dma("sync", sigz[qt],
                         self.d_sigz[:, cc * 256 + qt * 128:cc * 256 + (qt + 1) * 128],
                         transpose=True)
            self.dma("sync", st_bf, self.d_states[cc * 64:(cc + 1) * 64, :])
            self.w("vector", self.ld)
            if self.on("vector"):
                self.eng.tensor_copy(srun_bf, self.sb_srun[:])
                for qt in range(2):
                    tt = cc * 2 + qt
                    dv = self.sb_dtsp_tm[:, tt * 32:(tt + 1) * 32]
                    dv = dv.rearrange("p (h o) -> p h o", o=1).broadcast_to((128, 32, 128))
                    self.eng.tensor_tensor(
                        xdt[qt].rearrange("p (h q) -> p h q", h=32),
                        xctm[qt].rearrange("p (h q) -> p h q", h=32),
                        dv, ALU.mult)
                self.mark(self.ve)
            self.inc(self.ve, 1)
            self.w("tensor", self.ve)
            for hg in range(8):
                heads = list(range(hg * 4, hg * 4 + 4))
                for hi, h in enumerate(heads):
                    dAq = [self.sb_dA_tm[:, (cc * 2 + qt) * 32 + h:
                                         (cc * 2 + qt) * 32 + h + 1]
                           for qt in range(2)]
                    if self.on("vector"):
                        self.eng.tensor_scalar(band[:, 0:128],
                                               self.sb_consts[:, C_UT:C_UT + 128],
                                               dAq[0], None, ALU.mult)
                        self.eng.tensor_scalar(band[:, 128:256],
                                               self.sb_consts[:, C_ON:C_ON + 128],
                                               dAq[0], None, ALU.mult)
                        self.eng.tensor_scalar(band[:, 256:384],
                                               self.sb_consts[:, C_UT:C_UT + 128],
                                               dAq[1], None, ALU.mult)
                        self.mark(self.ve)
                    self.inc(self.ve, 1)
                    self.w("tensor", self.ve)
                    if self.on("tensor"):
                        LT = self.sb_consts_bf[:, C_LT:C_LT + 128]
                        ON = self.sb_consts_bf[:, C_ON:C_ON + 128]
                        sp = self.pb[7]
                        self.eng.matmul(sp[:, 0:128], LT, band[:, 0:128],
                                        start=True, stop=True,
                                        skip_group_check=True)
                        self.eng.matmul(sp[:, 128:256], LT, band[:, 128:256],
                                        start=True, stop=False,
                                        skip_group_check=True)
                        self.eng.matmul(sp[:, 128:256], ON, band[:, 256:384],
                                        start=False, stop=True,
                                        skip_group_check=True)
                        self.eng.matmul(sp[:, 256:384], LT, band[:, 256:384],
                                        start=True, stop=True,
                                        skip_group_check=True)
                        self.mark(self.pe)
                    self.inc(self.pe, 1)
                    self.w("scalar", self.pe)
                    if self.on("scalar"):
                        self.eng.activation(wexp, self.pb[7][:, 0:384], AF.Exp)
                        self.mark(self.ac)
                    self.inc(self.ac, 1)
                    self.w("vector", self.ac)
                    if self.on("vector"):
                        self.eng.tensor_tensor(
                            wsb, wexp,
                            self.sb_scoresm[:, cc * 384:(cc + 1) * 384], ALU.mult)
                        self.mark(self.ve)
                    self.inc(self.ve, 1)
                    self.w("tensor", self.ve)
                    hl = slice(h * 128, (h + 1) * 128)
                    if self.on("tensor"):
                        fst = (hi == 0)
                        yc0 = self.pb[0][:, hi * 128:(hi + 1) * 128]
                        yc1 = self.pb[1][:, hi * 128:(hi + 1) * 128]
                        self.eng.matmul(yc0, wsb[:, 0:128], xdt[0][:, hl],
                                        start=fst, stop=False,
                                        skip_group_check=True)
                        self.eng.matmul(yc0, self.sb_dskipI[:, hl],
                                        xctm[0][:, hl], start=False, stop=False,
                                        skip_group_check=True)
                        self.eng.matmul(yc1, wsb[:, 128:256], xdt[0][:, hl],
                                        start=fst, stop=False,
                                        skip_group_check=True)
                        self.eng.matmul(yc1, wsb[:, 256:384], xdt[1][:, hl],
                                        start=False, stop=False,
                                        skip_group_check=True)
                        self.eng.matmul(yc1, self.sb_dskipI[:, hl],
                                        xctm[1][:, hl], start=False, stop=False,
                                        skip_group_check=True)
                        for it in range(2):
                            self.eng.matmul(
                                self.pb[2 + it][:, hi * 128:(hi + 1) * 128],
                                self.sb_Ct0[0:64, cc * 256 + it * 128:
                                            cc * 256 + (it + 1) * 128],
                                srun_bf[0:64, hl],
                                start=fst, stop=(hi == 3),
                                skip_group_check=True)
                        self.mark(self.pe)
                    self.inc(self.pe, 1)
                self.w("scalar", self.pe)
                if self.on("scalar"):
                    self.eng.activation(rawsb[:, 0:512], self.pb[2][:, 0:512],
                                        AF.Copy)
                    self.eng.activation(rawsb[:, 512:1024], self.pb[3][:, 0:512],
                                        AF.Copy)
                    self.mark(self.ac)
                self.inc(self.ac, 1)
                for hi, h in enumerate(heads):
                    for it in range(2):
                        ecol = self.sb_expa_tm[:, (cc * 2 + it) * 32 + h:
                                               (cc * 2 + it) * 32 + h + 1]
                        self.w("vector", self.ac)
                        if self.on("vector"):
                            self.eng.tensor_scalar(
                                diagE, self.sb_consts[:, C_ID:C_ID + 128],
                                ecol, None, ALU.mult)
                            self.mark(self.ve)
                        self.inc(self.ve, 1)
                        self.w("tensor", self.ve)
                        if self.on("tensor"):
                            self.eng.matmul(
                                self.pb[it][:, hi * 128:(hi + 1) * 128],
                                diagE,
                                rawsb[:, it * 512 + hi * 128:
                                      it * 512 + (hi + 1) * 128],
                                start=False, stop=(hi == 3),
                                skip_group_check=True)
                            self.mark(self.pe)
                        self.inc(self.pe, 1)
                        self.w("vector", self.pe)
                self.w("vector", self.pe)
                gsl = slice(hg * 512, (hg + 1) * 512)
                if self.on("vector"):
                    for it in range(2):
                        self.eng.tensor_tensor(yg[it][:, gsl],
                                               self.pb[it][:, 0:512],
                                               sigz[it][:, gsl], ALU.mult)
                    self.mark(self.ve)
                self.inc(self.ve, 1)
                self.w("tensor", self.ve)
            for it in range(2):
                self.w("vector", self.ld)
                if self.on("vector"):
                    self.eng.tensor_tensor(ysq, yg[it], yg[it], ALU.mult)
                    self.eng.reduce_sum(rss[:, 0:32],
                                        ysq.rearrange("p (h q) -> p h q", h=32),
                                        axis=AX)
                    self.mark(self.ve)
                self.inc(self.ve, 1)
                self.w("scalar", self.ve)
                if self.on("scalar"):
                    self.eng.activation(rss[:, 32:64], rss[:, 0:32], AF.Sqrt,
                                        bias=self.sb_biasc[:, 1:2],
                                        scale=1.0 / 128.0)
                    self.mark(self.ac)
                self.inc(self.ac, 1)
                self.w("vector", self.ac)
                if self.on("vector"):
                    self.eng.reciprocal(rss[:, 64:96], rss[:, 32:64])
                    rv = rss[:, 64:96].rearrange("p (h o) -> p h o", o=1)
                    rv = rv.broadcast_to((128, 32, 128))
                    self.eng.tensor_tensor(
                        ynbf.rearrange("p (h q) -> p h q", h=32),
                        yg[it].rearrange("p (h q) -> p h q", h=32), rv, ALU.mult)
                    self.mark(self.ve)
                self.inc(self.ve, 1)
                self.w("sync", self.ve)
                self.dma("sync",
                         self.d_yn[cc * 256 + it * 128:cc * 256 + (it + 1) * 128, :],
                         ynbf)
            if self.on("vector"):
                for h in range(NHEADS):
                    hl = slice(h * 128, (h + 1) * 128)
                    self.eng.scalar_tensor_tensor(
                        self.sb_srun[:, hl], self.sb_srun[:, hl],
                        self.sb_decrep[:, cc * 32 + h:cc * 32 + h + 1],
                        st_bf[:, hl], ALU.mult, ALU.add)
                self.mark(self.ve)
            self.inc(self.ve, 1)
            self.w("sync", self.ve)
        self.barrier()

    def s_oproj(self):
        A = 40
        ynT = [self.av(A + f * 512, 128, 1024, BF16) for f in range(32)]
        A2 = A + 32 * 512
        wo_buf = [self.av(A2 + i * 256, 128, 512, BF16) for i in range(2)]
        ofm = self.av(A2 + 520, 128, 4096, BF16)
        for f in range(32):
            self.dma("sync", ynT[f], self.d_yn[:, f * 128:(f + 1) * 128],
                     transpose=True)
        self.w("vector", self.ld)
        if self.on("vector"):
            for f in range(32):
                self.eng.tensor_scalar(ynT[f], ynT[f], self.sb_normw[:, f:f + 1],
                                       None, ALU.mult)
            self.mark(self.ve)
        self.inc(self.ve, 1)
        self.w("tensor", self.ve)
        self.w("gpsimd", self.ve)
        for eg in range(4):
            for f in range(32):
                wbuf = wo_buf[f % 2]
                if eg > 0 or f >= 2:
                    self.w("gpsimd", self.pe)
                wsem = self.wsA if f % 2 == 0 else self.wsB
                self.dma("gpsimd", wbuf,
                         self.i_wo[f * 128:(f + 1) * 128,
                                   eg * 512:(eg + 1) * 512], sem=wsem)
                self.w("tensor", wsem)
                if eg > 0 and f == 0:
                    self.w("tensor", self.ac)
                for el in range(4):
                    for th in range(2):
                        if self.on("tensor"):
                            self.eng.matmul(
                                self.pb[el * 2 + th][:, 0:512],
                                wbuf[:, el * 128:(el + 1) * 128],
                                ynT[f][:, th * 512:(th + 1) * 512],
                                start=(f == 0), stop=(f == 31),
                                skip_group_check=True)
                if self.on("tensor"):
                    self.mark(self.pe)
                self.inc(self.pe, 1)
            self.w("scalar", self.pe)
            self.w("scalar", self.ld)
            if self.on("scalar"):
                for el in range(4):
                    for th in range(2):
                        self.eng.activation(
                            ofm[:, el * 1024 + th * 512:el * 1024 + (th + 1) * 512],
                            self.pb[el * 2 + th][:, 0:512], AF.Copy)
                self.mark(self.ac)
            self.inc(self.ac, 1)
            self.w("sync", self.ac)
            for el in range(4):
                self.dma("sync",
                         self.d_attnfm[eg * 512 + el * 128:
                                       eg * 512 + (el + 1) * 128, :],
                         ofm[:, el * 1024:(el + 1) * 1024])
        self.barrier()

    def s_attnep(self):
        A = 40
        self.x2_off = 14500
        at = self.av(A, 128, 2048, BF16)
        ut = self.av(A + 1030, 128, 2048)
        sq = self.av(A + 3080, 128, 2048)
        stats = self.av(A + 5130, 128, 28)
        tmp = self.av(A + 5160, 128, 2048)
        xn2 = self.av(A + 7210, 128, 2048, BF16)
        for t in range(8):
            x2 = self.av(self.x2_off + t * 1024, 128, 2048, BF16)
            self.dma("sync", at, self.d_attnfm[:, t * 128:(t + 1) * 128],
                     transpose=True)
            self.dma("sync", ut, self.i_u[128 + t * 128:128 + (t + 1) * 128, :])
            self.w("vector", self.ld)
            self.w("scalar", self.ld)
            self._ln_small(at, sq, stats, 2048, EPS_LN)
            if self.on("vector"):
                self.eng.tensor_scalar(tmp, at, stats[:, 4:5], stats[:, 24:25],
                                       ALU.subtract, ALU.mult)
                self.eng.tensor_add(x2, tmp, ut)
                self.mark(self.ve)
            self.inc(self.ve, 1)
            self.w("scalar", self.ve)
            self._ln_small(x2, sq, stats, 2048, EPS_LN)
            if self.on("vector"):
                self.eng.tensor_scalar(xn2, x2, stats[:, 4:5], stats[:, 24:25],
                                       ALU.subtract, ALU.mult).then_inc(self.ve, 1)
            self.inc(self.ve, 1)
            self.w("sync", self.ve)
            self.dma("sync", self.d_xn2[t * 128:(t + 1) * 128, :], xn2)
        self.barrier()

    def s_ffn_gateup(self):
        A = 40
        xn2T = [self.av(A + k * 512, 128, 1024, BF16) for k in range(16)]
        wbuf = [self.av(22700 + i * 2048, 128, 16 * 128, BF16) for i in range(4)]
        sil = self.av(8300, 128, 1024)
        gup = self.av(9330, 128, 1024, BF16)
        for k in range(16):
            self.dma("sync", xn2T[k], self.d_xn2[:, k * 128:(k + 1) * 128],
                     transpose=True)
        self.w("tensor", self.ld)
        self.w("gpsimd", self.ld)
        for h in range(64):
            wg = wbuf[(h % 2) * 2]
            wu = wbuf[(h % 2) * 2 + 1]
            if h >= 2:
                self.w("gpsimd", self.pe)
            src_g = self.i_wg[h * 2048:(h + 1) * 2048, :]
            src_u = self.i_wu[h * 2048:(h + 1) * 2048, :]
            wsem = self.wsA if h % 2 == 0 else self.wsB
            self.dma("gpsimd", wg.rearrange("p (k e) -> p k e", k=16),
                     src_g.rearrange("(k p) e -> p k e", p=128), sem=wsem)
            self.dma("gpsimd", wu.rearrange("p (k e) -> p k e", k=16),
                     src_u.rearrange("(k p) e -> p k e", p=128), sem=wsem)
            self.w("tensor", wsem)
            bb = (h % 2) * 4
            if h >= 2:
                self.w("tensor", self.ve)
            for k in range(16):
                for th in range(2):
                    if self.on("tensor"):
                        self.eng.matmul(self.pb[bb + th][:, 0:512],
                                        wg[:, k * 128:(k + 1) * 128],
                                        xn2T[k][:, th * 512:(th + 1) * 512],
                                        start=(k == 0), stop=(k == 15),
                                        skip_group_check=True)
                        self.eng.matmul(self.pb[bb + 2 + th][:, 0:512],
                                        wu[:, k * 128:(k + 1) * 128],
                                        xn2T[k][:, th * 512:(th + 1) * 512],
                                        start=(k == 0), stop=(k == 15),
                                        skip_group_check=True)
            if self.on("tensor"):
                self.mark(self.pe)
            self.inc(self.pe, 1)
            self.w("scalar", self.pe)
            if self.on("scalar"):
                self.eng.activation(sil[:, 0:512], self.pb[bb][:, 0:512], AF.Silu)
                self.eng.activation(sil[:, 512:1024], self.pb[bb + 1][:, 0:512],
                                    AF.Silu)
                self.mark(self.ac)
            self.inc(self.ac, 1)
            self.w("vector", self.ac)
            self.w("vector", self.ld)
            if self.on("vector"):
                self.eng.tensor_tensor(gup[:, 0:512], self.pb[bb + 2][:, 0:512],
                                       sil[:, 0:512], ALU.mult)
                self.eng.tensor_tensor(gup[:, 512:1024],
                                       self.pb[bb + 3][:, 0:512],
                                       sil[:, 512:1024], ALU.mult)
                self.mark(self.ve)
            self.inc(self.ve, 1)
            self.w("sync", self.ve)
            self.dma("sync", self.d_gup[h * 128:(h + 1) * 128, :], gup)
        self.barrier()

    def s_ffn_down(self):
        A = 40
        gupk = [self.av(A + i * 512, 128, 1024, BF16) for i in range(2)]
        wdb = [self.av(A + 1024 + i * 256, 128, 512, BF16) for i in range(2)]
        self.ffo_off = A + 1640
        for gen in range(4):
            for k in range(64):
                gb = gupk[k % 2]
                wb = wdb[k % 2]
                if gen > 0 or k >= 2:
                    self.w("sync", self.pe)
                    self.w("gpsimd", self.pe)
                gsem = self.gbA if k % 2 == 0 else self.gbB
                wsem = self.wsA if k % 2 == 0 else self.wsB
                self.dma("sync", gb, self.d_gup[k * 128:(k + 1) * 128, :],
                         sem=gsem)
                self.dma("gpsimd", wb,
                         self.i_wd[k * 128:(k + 1) * 128,
                                   gen * 512:(gen + 1) * 512], sem=wsem)
                self.w("tensor", gsem)
                self.w("tensor", wsem)
                if gen > 0 and k == 0:
                    self.w("tensor", self.ac)
                for t in range(8):
                    if self.on("tensor"):
                        self.eng.matmul(self.pb[t][:, 0:512],
                                        gb[:, t * 128:(t + 1) * 128], wb,
                                        start=(k == 0), stop=(k == 63),
                                        skip_group_check=True)
                if self.on("tensor"):
                    self.mark(self.pe)
                self.inc(self.pe, 1)
            self.w("scalar", self.pe)
            if self.on("scalar"):
                for t in range(8):
                    dst = self.av(self.ffo_off + t * 1024, 128, 2048, BF16)
                    self.eng.activation(dst[:, gen * 512:(gen + 1) * 512],
                                        self.pb[t][:, 0:512], AF.Copy)
                self.mark(self.ac)
            self.inc(self.ac, 1)
        self.barrier()

    def s_dbgdump(self):
        for name, spec in DEBUG_TAPS:
            self.dma("sync", self.o_dbg[name], spec["src"](self))
        self.w("sync", self.ld)
        self.barrier()

    def s_final(self):
        A2 = self.ffo_off + 8 * 1024
        sq = self.av(A2, 128, 2048)
        stats = self.av(A2 + 2060, 128, 28)
        outt = self.av(A2 + 2090, 128, 2048)
        for t in range(8):
            ff = self.av(self.ffo_off + t * 1024, 128, 2048, BF16)
            x2 = self.av(self.x2_off + t * 1024, 128, 2048, BF16)
            if t > 0:
                self.w("vector", self.ld)
            self._ln_small(ff, sq, stats, 2048, EPS_LN)
            if self.on("vector"):
                self.eng.tensor_scalar(outt, ff, stats[:, 4:5], stats[:, 24:25],
                                       ALU.subtract, ALU.mult)
                self.eng.tensor_add(outt, outt, x2).then_inc(self.ve, 1)
            self.inc(self.ve, 1)
            self.w("sync", self.ve)
            self.dma("sync", self.o_out[t * 128:(t + 1) * 128, :], outt)
        self.w("sync", self.ld)
        self.barrier()

    def build(self):
        self.declare()
        with contextlib.ExitStack() as es:
            self.alloc(es)
            block = es.enter_context(self.nc.Block())
            stages = [self.s_setup,
                      lambda: self.s_ln1(True),
                      lambda: self.s_ln1(False),
                      lambda: self.s_mixer_front(True),
                      lambda: self.s_mixer_front(False),
                      self.s_exchange,
                      self.s_phase2,
                      self.s_oproj,
                      self.s_attnep,
                      self.s_ffn_gateup,
                      self.s_ffn_down,
                      self.s_final]
            if STAGE_LIMIT is not None:
                stages = stages[:STAGE_LIMIT]
            if SKIP_LN:
                stages = [s for i, s in enumerate(stages) if i not in (1, 2)]
            if SKIP_SETUP:
                stages = stages[1:]
            stages = stages + [self.s_dbgdump]

            def run_all(engname, eng):
                self.E = engname
                for _s in (self.wsA, self.wsB, self.gbA, self.gbB):
                    self.c[_s.name] = 0
                fsem = {"vector": self.f_ve, "scalar": self.f_ac,
                        "gpsimd": self.f_po, "sync": self.f_sy,
                        "tensor": self.f_ve}[engname]
                self.eng = EngProxy(eng, fsem, engname == "tensor",
                                    real_nop_fn=self.real_nop)
                self.stage_no = 0
                for s in (self.ld, self.gd, self.pe, self.ve, self.ac,
                          self.po, self.bar, self.pol):
                    self.c[s.name] = 0
                for st in stages:
                    st()

            block.sync(lambda eng: run_all("sync", eng))
            block.tensor(lambda eng: run_all("tensor", eng))
            block.vector(lambda eng: run_all("vector", eng))
            block.scalar(lambda eng: run_all("scalar", eng))
            block.gpsimd(lambda eng: run_all("gpsimd", eng))
        return self.nc


_NC_CACHE = {}


def _build_cached():
    key = ("v1", STAGE_LIMIT, MF_PARTS, MF_SUB, SKIP_LN, SKIP_SETUP, tuple(sorted(MF_FLAGS)), tuple(n for n, _ in DEBUG_TAPS))
    if key not in _NC_CACHE:
        _NC_CACHE[key] = KB().build()
    return _NC_CACHE[key]


def _make_consts():
    c = np.zeros((128, 768), np.float32)
    c[:, C_ID:C_ID + 128] = np.eye(128)
    q = np.arange(128)
    c[:, C_LT:C_LT + 128] = (q[:, None] > q[None, :]).astype(np.float32)
    c[:, C_ON:C_ON + 128] = 1.0
    c[:, C_UT:C_UT + 128] = (q[:, None] <= q[None, :]).astype(np.float32)
    return c


def kernel(L0_attn_u, L0_attn_in_proj_w, L0_attn_conv_w, L0_attn_conv_b,
           L0_attn_A_log, L0_attn_dt_bias, L0_attn_D_skip, L0_attn_norm_w,
           L0_attn_o_proj_w, L0_ffn_w_gate, L0_ffn_w_up, L0_ffn_w_down):
    u = np.ascontiguousarray(np.asarray(L0_attn_u, np.float32))
    w_in = np.asarray(L0_attn_in_proj_w, np.float32)
    w_in_t = np.zeros((EMBED, E_PAD), np.float32)
    w_in_t[:, :D_IN_PROJ] = w_in.T
    conv_w = np.asarray(L0_attn_conv_w, np.float32)
    conv_w_r = np.ascontiguousarray(
        conv_w.reshape(33, 128, 4).transpose(1, 0, 2).reshape(128, 132))
    conv_b_r = np.ascontiguousarray(
        np.asarray(L0_attn_conv_b, np.float32).reshape(33, 128).T)
    a_log = np.ascontiguousarray(np.asarray(L0_attn_A_log, np.float32)[:, None])
    dt_bias = np.ascontiguousarray(
        np.asarray(L0_attn_dt_bias, np.float32)[:, None])
    dskip_row = np.ascontiguousarray(
        np.asarray(L0_attn_D_skip, np.float32)[None, :])
    norm_w_r = np.ascontiguousarray(
        np.asarray(L0_attn_norm_w, np.float32).reshape(32, 128).T)
    w_o_t = np.ascontiguousarray(np.asarray(L0_attn_o_proj_w, np.float32).T)
    wg_blk = np.ascontiguousarray(
        np.asarray(L0_ffn_w_gate, np.float32).T.reshape(EMBED, 64, 128)
        .transpose(1, 0, 2).reshape(64 * EMBED, 128))
    wu_blk = np.ascontiguousarray(
        np.asarray(L0_ffn_w_up, np.float32).T.reshape(EMBED, 64, 128)
        .transpose(1, 0, 2).reshape(64 * EMBED, 128))
    w_d_t = np.ascontiguousarray(np.asarray(L0_ffn_w_down, np.float32).T)
    consts = _make_consts()

    nonce = (int(time.time_ns()) & 0x3FFFFFFF) | 1
    in_maps = []
    for k in range(8):
        s, p = k // 4, k % 4
        um = np.zeros((SEGH, EMBED), np.float32)
        um[128:] = u[s, p * 1024:(p + 1) * 1024]
        if p > 0:
            um[:128] = u[s, p * 1024 - 128:p * 1024]
        uh = np.zeros((SEGH, EMBED), np.float32)
        if p == 2:
            uh[128:] = u[s, 0:1024]
        elif p == 3:
            uh[128:] = u[s, 1024:2048]
            uh[:128] = u[s, 896:1024]
        maskc = np.zeros((1, 8), np.float32)
        if p == 1:
            maskc[0, 2] = 1.0
        elif p == 2:
            maskc[0, 0] = 1.0
            maskc[0, 1] = 1.0
        elif p == 3:
            maskc[0, 3] = 1.0
        ctl = np.array([[k % 2, nonce, nonce + 1, 0]], np.int32)
        in_maps.append(dict(
            u_main=um, u_hist=uh, w_in_t=w_in_t, conv_w_r=conv_w_r,
            conv_b_r=conv_b_r, a_log=a_log, dt_bias=dt_bias,
            norm_w_r=norm_w_r, dskip_row=dskip_row, w_o_t=w_o_t,
            wg_blk=wg_blk, wu_blk=wu_blk, w_d_t=w_d_t, consts=consts,
            maskc=maskc, ctl=ctl))

    nc = _build_cached()
    res = run_bass_kernel_spmd(nc, in_maps, core_ids=list(range(8)))
    outs = [np.asarray(res.results[k]["out"], np.float32) for k in range(8)]
    full = np.stack(outs).reshape(2, 4096, 2048)
    kernel.debug_results = res.results
    return full
